# version 06 (final)
# baseline (speedup 1.0000x reference)
import os
import numpy as np
import ml_dtypes
from contextlib import ExitStack

import concourse.bass as bass
import concourse.mybir as mybir
from concourse.bass_utils import run_bass_kernel_spmd

F32 = mybir.dt.float32
BF16 = mybir.dt.bfloat16
U8 = mybir.dt.uint8
ALU = mybir.AluOpType
ACT = mybir.ActivationFunctionType
AX = mybir.AxisListType

D = 1024
DC = 8
L = 4096
NST = 32
Q = 127
SEG = 128
NCH = 33
LASTW = L - 32 * Q
TW = 512
NT = L // TW
FF = 4096
NFT = 32
CUBE = NST * SEG
SEXT = NCH * SEG
MAGIC = float(1.5 * 2.0 ** 23)
NEG_INF = -3.0e38
ARENA = 52900
MIXTAIL = 44000
ENGS = ("sync", "vector", "scalar", "tensor", "gpsimd")
MAX_PH = int(os.environ.get("KMAX_PH", "99"))
DIS_CC = int(os.environ.get("KDIS_CC", "0"))
SKIP_LNMAX = int(os.environ.get("KSKIP_LNMAX", "0"))
CC_EARLY = int(os.environ.get("KCC_EARLY", "0"))
PH0_MASK = int(os.environ.get("KPH0_MASK", "31"))
KDBG = int(os.environ.get("KDBG", "0"))
GSM_MASK = int(os.environ.get("KGSM_MASK", "15"))


class Carver:
    def __init__(self, arena, lo, hi):
        self.arena = arena
        self.off = lo
        self.hi = hi
        self.marks = []

    def push(self):
        self.marks.append(self.off)

    def pop(self):
        self.off = self.marks.pop()

    def f32(self, n):
        o = self.off
        assert o + n <= self.hi, f"arena overflow {o}+{n} > {self.hi}"
        self.off += n
        return self.arena[:, o:o + n]

    def bf16(self, n):
        k = (n + 1) // 2
        return self.f32(k).bitcast(BF16)[:, :n]

    def u8(self, n):
        k = (n + 3) // 4
        return self.f32(k).bitcast(U8)[:, :n]


def build_program():
    nc = bass.Bass("TRN2", target_bir_lowering=False)

    d_ccin = nc.dram_tensor("ccin", [1, 4], F32, kind="Internal").ap()
    d_ccout = nc.dram_tensor("ccout", [1, 4], F32, kind="Internal").ap()
    t_x = nc.dram_tensor("x", [D, L], F32, kind="ExternalInput").ap()
    t_winT = nc.dram_tensor("winT", [D, 2112], F32, kind="ExternalInput").ap()
    t_woutT = nc.dram_tensor("woutT", [D, D], F32, kind="ExternalInput").ap()
    t_w1 = nc.dram_tensor("w1", [FF, D], F32, kind="ExternalInput").ap()
    t_w1T = nc.dram_tensor("w1T", [D, FF], F32, kind="ExternalInput").ap()
    t_w2 = nc.dram_tensor("w2", [D, FF], F32, kind="ExternalInput").ap()
    t_w2T = nc.dram_tensor("w2T", [FF, D], F32, kind="ExternalInput").ap()
    t_alogT = nc.dram_tensor("alogT", [D, NST], F32, kind="ExternalInput").ap()
    t_convw = nc.dram_tensor("convw", [D, 4], F32, kind="ExternalInput").ap()
    t_convb = nc.dram_tensor("convb", [D], F32, kind="ExternalInput").ap()
    t_dp = nc.dram_tensor("dp", [D], F32, kind="ExternalInput").ap()
    t_b1 = nc.dram_tensor("b1", [FF], F32, kind="ExternalInput").ap()
    t_b2 = nc.dram_tensor("b2", [D], F32, kind="ExternalInput").ap()
    t_onesbf = nc.dram_tensor("onesbf", [128, 1], BF16, kind="ExternalInput").ap()
    t_idf = nc.dram_tensor("idf", [128, 128], F32, kind="ExternalInput").ap()
    t_out = nc.dram_tensor("out", [D, L], F32, kind="ExternalOutput").ap()

    KI = "ExternalOutput" if KDBG else "Internal"
    d_uscr = nc.dram_tensor("uscr", [DC, 128, SEXT], F32, kind=KI).ap()
    d_sscr = nc.dram_tensor("sscr", [DC, 128, SEXT], F32, kind=KI).ap()
    d_bcscr = nc.dram_tensor("bcscr", [64, SEXT], BF16, kind=KI).ap()
    d_yscr = nc.dram_tensor("yscr", [DC, 128, L], F32, kind=KI).ap()
    d_x2scr = nc.dram_tensor("x2scr", [DC, 128, L], F32, kind=KI).ap()
    d_qxscr = nc.dram_tensor("qxscr", [DC, 128, L], U8, kind=KI).ap()
    d_rows = nc.dram_tensor("rows", [4, L], F32, kind=KI).ap()
    d_bcflat = nc.dram_tensor("bcflat", [2, CUBE], BF16, kind="Internal").ap()
    d_small = nc.dram_tensor("small", [2, FF], F32, kind="Internal").ap()
    d_dbg = nc.dram_tensor("dbg", [8, FF], F32, kind="ExternalOutput" if KDBG else "Internal").ap()

    with ExitStack() as ctx:
        arena_t = ctx.enter_context(nc.sbuf_tensor("arena", [128, ARENA], F32))
        arena = arena_t[:, :]

        CP = Carver(arena, 0, ARENA)
        zeros = CP.f32(128)
        idf = CP.f32(128)
        b1c = CP.f32(NFT)
        b2c = CP.f32(DC)
        sw1 = CP.f32(NFT)
        sw2 = CP.f32(DC)
        alpha1 = CP.f32(NFT)
        bq1 = CP.f32(NFT)
        scl2 = CP.f32(NFT)
        bia2 = CP.f32(NFT)
        alpha2 = CP.f32(DC)
        bq2 = CP.f32(DC)
        tmp32a = CP.f32(NFT)
        tmp32b = CP.f32(NFT)
        tiny = CP.f32(16)
        s1col = tiny[:, 0:1]; is1col = tiny[:, 1:2]
        s2col = tiny[:, 2:3]; is2col = tiny[:, 3:4]
        lnmax = tiny[:, 4:5]; hmax = tiny[:, 5:6]
        gmaxc = tiny[:, 6:7]
        mrow4 = tiny[:, 8:12]
        onesbf = CP.f32(1).bitcast(BF16)[:, 0:1]
        core_end = CP.off

        MT = Carver(arena, MIXTAIL, ARENA)
        acol = MT.f32(DC * NST)
        convw_c = MT.f32(32)
        convb_c = MT.f32(DC)
        dp_c = MT.f32(DC)
        resetf = MT.f32(SEXT)
        hstate = MT.f32(DC * NST)
        woutTb = MT.bf16(DC * 1024)
        assert MT.off <= ARENA, MT.off

        psA = [ctx.enter_context(nc.psum_tensor(f"ps{i}", [128, 512], F32))
               for i in range(4)]
        pr0 = ctx.enter_context(nc.psum_tensor("pr0", [1, 512], F32))
        pr1 = ctx.enter_context(nc.psum_tensor("pr1", [1, 512], F32))
        prt = ctx.enter_context(nc.psum_tensor("prt", [1, 128], F32))

        sem_names = ["bar", "ld0", "ldx", "rowdma", "bcast", "smm", "sfree",
                     "evac", "aux", "aux2", "sxm", "se", "sdc", "cld", "cbf",
                     "ybst", "sstore", "ustore", "wld", "qhs", "cc", "xlf",
                     "sout", "qld"]
        sems = {n: ctx.enter_context(nc.semaphore(n)) for n in sem_names}
        block = ctx.enter_context(nc.Block())

        def make_plan(eng, E):
            C = {n: 0 for n in sem_names}
            nbar = [0]
            ee = E[eng]

            def inc(s, n=1):
                C[s] += n
                return C[s]

            def on(e):
                return eng == e

            def dma(q, out, in_, s, slow=False):
                tgt = inc(s, 16)
                if on(q):
                    ee.dma_start(out=out, in_=in_,
                                 allow_slow_non_contiguous=slow
                                 ).then_inc(sems[s], 16)
                return tgt

            def wait(we, s, val):
                if on(we) and val > 0:
                    ee.wait_ge(sems[s], val)

            def sig(we, s, n=1, li=None):
                tgt = inc(s, n)
                if on(we):
                    (li if li is not None else ee.drain()).then_inc(sems[s], n)
                return tgt

            def barrier():
                nbar[0] += 5
                ee.drain().then_inc(sems["bar"], 1)
                ee.wait_ge(sems["bar"], nbar[0])

            A = Carver(arena, core_end, MIXTAIL)
            gidx = [0]

            def mm_group(lhs_fn, rhs_fn, nk, width=128, ncols=TW):
                g = gidx[0]
                gidx[0] += 1
                pt = psA[g % 4]
                wait("tensor", "sfree", g - 3)
                mi = None
                if on("tensor"):
                    t = E["tensor"]
                    for k in range(nk):
                        mi = t.matmul(pt[0:width, 0:ncols], lhs_fn(k), rhs_fn(k),
                                      start=(k == 0), stop=(k == nk - 1))
                sig("tensor", "smm", li=mi)
                return pt

            def pfree(we):
                sig(we, "sfree")

            def global_scalar_max(local_col, scol, icol):
                if GSM_MASK & 1:
                    t = sig("vector", "evac")
                    wait("tensor", "evac", t)
                    mi = None
                    if on("tensor"):
                        mi = E["tensor"].transpose(prt[0:1, :], local_col, idf)
                    t = sig("tensor", "aux", li=mi)
                    wait("vector", "aux", t)
                    if on("vector"):
                        v = E["vector"]
                        v.reduce_max(mrow4[0:1, 0:1], prt[0:1, :], AX.X)
                        v.drain()
                        v.tensor_copy(mrow4[0:1, 1:4],
                                      mrow4[0:1, 0:1].broadcast_to([1, 3]))
                if GSM_MASK & 2:
                    t = sig("vector", "evac")
                    wait("sync", "evac", t)
                    dma("sync", d_ccin, mrow4[0:1, 0:4], "cbf")
                    wait("gpsimd", "cbf", C["cbf"])
                    ci = None
                    if on("gpsimd") and not DIS_CC:
                        ci = E["gpsimd"].collective_compute(
                            "AllReduce", ALU.max,
                            replica_groups=[[0, 1, 2, 3, 4, 5, 6, 7]],
                            ins=[d_ccin], outs=[d_ccout])
                    sig("gpsimd", "cc", li=ci)
                if GSM_MASK & 4:
                    wait("sync", "cc", C["cc"])
                    dma("sync", gmaxc,
                        d_ccout[0, 0:1].unsqueeze(0).broadcast_to([128, 1]),
                        "bcast")
                if GSM_MASK & 8:
                    wait("vector", "bcast", C["bcast"])
                    if on("vector"):
                        v = E["vector"]
                        v.tensor_scalar(scol, gmaxc, 1e-8, 1.0 / 255.0,
                                        ALU.max, ALU.mult)
                        v.drain()
                        v.reciprocal(icol, scol)
                return sig("vector", "aux2")


            def rows_math(bufs, row_base, tt):
                rowm, rowv, rowt = bufs
                if on("vector"):
                    v = E["vector"]
                    m = rowm[0:1, :]
                    va = rowv[0:1, :]
                    v.tensor_scalar(m, pr0[0:1, :], 1.0 / D, None, ALU.mult)
                    v.tensor_scalar(va, pr1[0:1, :], 1.0 / D, None, ALU.mult)
                    v.drain()
                    v.tensor_tensor(rowt[0:1, :], m, m, ALU.mult)
                    v.tensor_sub(va, va, rowt[0:1, :])
                    v.tensor_scalar(va, va, 1e-5, None, ALU.add)
                t = sig("vector", "evac")
                wait("scalar", "evac", t)
                if on("scalar"):
                    E["scalar"].activation(rowv[0:1, :], rowv[0:1, :], ACT.Sqrt)
                t = sig("scalar", "aux")
                wait("vector", "aux", t)
                if on("vector"):
                    E["vector"].drain()
                    E["vector"].reciprocal(rowt[0:1, :], rowv[0:1, :])
                t = sig("vector", "evac")
                wait("scalar", "evac", t)
                dma("scalar", d_rows[row_base, tt * TW:(tt + 1) * TW].unsqueeze(0),
                    rowm[0:1, :], "rowdma")
                dma("scalar", d_rows[row_base + 1, tt * TW:(tt + 1) * TW].unsqueeze(0),
                    rowt[0:1, :], "rowdma")

            def stats_tile(bufs, xsrc, tt, row_base):
                xbb, xsq, rowm, rowv, rowt = bufs
                s0 = C["smm"]
                for dc in range(DC):
                    pb = dc % 2
                    wait("vector", "smm", s0 + dc - 1)
                    if on("vector"):
                        v = E["vector"]
                        v.tensor_copy(xbb[pb], xsrc(dc))
                        v.tensor_tensor(xsq[pb], xbb[pb], xbb[pb], ALU.mult)
                    t = sig("vector", "evac")
                    wait("tensor", "evac", t)
                    if dc == 0:
                        wait("tensor", "aux2", C["aux2"])
                    mi = None
                    if on("tensor"):
                        tn = E["tensor"]
                        tn.matmul(pr0[0:1, :], onesbf, xbb[pb],
                                  start=(dc == 0), stop=(dc == DC - 1))
                        mi = tn.matmul(pr1[0:1, :], onesbf, xsq[pb],
                                       start=(dc == 0), stop=(dc == DC - 1))
                    sig("tensor", "smm", li=mi)
                wait("vector", "smm", C["smm"])
                rows_math((rowm, rowv, rowt), row_base, tt)
                sig("vector", "aux2")

            A.push()
            atmp = A.f32(DC * NST)
            if not (PH0_MASK & 1):
                dma("sync", idf, t_idf, "ld0")
                barrier()
                A.pop()
                if MAX_PH <= 0:
                    if CC_EARLY:
                        global_scalar_max(lnmax, s1col, is1col)
                        barrier()
                    return C
            dma("sync", idf, t_idf, "ld0")
            dma("sync", onesbf, t_onesbf, "ld0")
            for dc in range(DC):
                dma("sync", atmp[:, dc * NST:(dc + 1) * NST],
                    t_alogT[dc * 128:(dc + 1) * 128, :], "ld0")
                dma("sync", convw_c[:, dc * 4:dc * 4 + 4],
                    t_convw[dc * 128:(dc + 1) * 128, :], "ld0")
                dma("sync", convb_c[:, dc:dc + 1],
                    t_convb[dc * 128:(dc + 1) * 128].unsqueeze(1), "ld0")
                dma("sync", dp_c[:, dc:dc + 1],
                    t_dp[dc * 128:(dc + 1) * 128].unsqueeze(1), "ld0")
                dma("sync", b2c[:, dc:dc + 1],
                    t_b2[dc * 128:(dc + 1) * 128].unsqueeze(1), "ld0")
            for ft in range(NFT):
                dma("sync", b1c[:, ft:ft + 1],
                    t_b1[ft * 128:(ft + 1) * 128].unsqueeze(1), "ld0")
            if PH0_MASK & 2:
                for k in range(DC):
                    dma("gpsimd", woutTb[:, k * 1024:(k + 1) * 1024],
                        t_woutT[k * 128:(k + 1) * 128, :], "ld0")
            ld0_tot = C["ld0"]
            if on("vector"):
                v = E["vector"]
                v.memset(zeros, 0.0)
                v.memset(hstate, 0.0)
                v.memset(resetf, 1.0)
                v.memset(resetf.rearrange("p (c s) -> p c s", s=SEG)[:, :, 0:1], 0.0)
                v.memset(lnmax, NEG_INF)
                v.memset(hmax, NEG_INF)
            wait("scalar", "ld0", ld0_tot)
            if on("scalar"):
                E["scalar"].activation(acol, atmp, ACT.Exp)
            t0 = sig("scalar", "aux")
            wait("vector", "aux", t0)
            wait("vector", "ld0", ld0_tot)
            if on("vector"):
                E["vector"].tensor_scalar_mul(acol, acol, -1.0)
            t1 = sig("vector", "aux")
            wait("sync", "aux", t1)
            for dc in range(DC):
                dma("sync", d_uscr[dc, :, 0:1], zeros[:, 0:1], "sstore",
                    slow=True)
                dma("sync", d_uscr[dc, :, 1 + L:SEXT],
                    zeros[:, 0:SEXT - 1 - L], "sstore")
            dma("sync", d_bcscr[:, L:SEXT],
                zeros[0:64, 0:(SEXT - L) // 2].bitcast(BF16)[:, 0:SEXT - L],
                "sstore")
            barrier()
            A.pop()
            if MAX_PH <= 0:
                if CC_EARLY:
                    global_scalar_max(lnmax, s1col, is1col)
                    barrier()
                return C

            A.push()
            xbuf = [A.f32(L), A.f32(L)]
            sbufs = ([A.bf16(TW), A.bf16(TW)], [A.bf16(TW), A.bf16(TW)],
                     A.f32(TW), A.f32(TW), A.f32(TW))
            sbufs = (sbufs[0], sbufs[1], sbufs[2], sbufs[3], sbufs[4])
            xlf_b = C["xlf"]
            for tt in range(NT):
                wait("sync", "xlf", xlf_b + tt - 1)
                for dc in range(DC):
                    dma("sync", xbuf[tt % 2][:, dc * TW:(dc + 1) * TW],
                        t_x[dc * 128:(dc + 1) * 128, tt * TW:(tt + 1) * TW],
                        "ldx")
                wait("vector", "ldx", C["ldx"])
                stats_tile(sbufs,
                           lambda dc: xbuf[tt % 2][:, dc * TW:(dc + 1) * TW],
                           tt, 0)
                sig("vector", "xlf")
            barrier()
            A.pop()
            if MAX_PH <= 1:
                if CC_EARLY:
                    global_scalar_max(lnmax, s1col, is1col)
                    barrier()
                return C

            A.push()
            xln = A.bf16(DC * L)
            A.push()
            xbuf = [A.f32(L), A.f32(L)]
            mrep = A.f32(L)
            irep = A.f32(L)
            ttmp = A.f32(TW)
            wait("sync", "rowdma", C["rowdma"])
            dma("sync", mrep, d_rows[0, :].unsqueeze(0).broadcast_to([128, L]),
                "bcast")
            dma("sync", irep, d_rows[1, :].unsqueeze(0).broadcast_to([128, L]),
                "bcast")
            xv = xln.rearrange("p (a t) -> p a t", a=DC)
            xlf_b = C["xlf"]
            for tt in range(NT):
                wait("sync", "xlf", xlf_b + tt - 1)
                for dc in range(DC):
                    dma("sync", xbuf[tt % 2][:, dc * TW:(dc + 1) * TW],
                        t_x[dc * 128:(dc + 1) * 128, tt * TW:(tt + 1) * TW],
                        "ldx")
                wait("vector", "ldx", C["ldx"])
                wait("vector", "bcast", C["bcast"])
                for dc in range(DC):
                    if on("vector"):
                        v = E["vector"]
                        xt = xbuf[tt % 2][:, dc * TW:(dc + 1) * TW]
                        v.tensor_sub(ttmp, xt, mrep[:, tt * TW:(tt + 1) * TW])
                        v.tensor_tensor(xv[:, dc, tt * TW:(tt + 1) * TW],
                                        ttmp, irep[:, tt * TW:(tt + 1) * TW],
                                        ALU.mult)
                sig("vector", "xlf")
            barrier()
            A.pop()
            if MAX_PH <= 2:
                if CC_EARLY:
                    global_scalar_max(lnmax, s1col, is1col)
                    barrier()
                return C

            A.push()
            wchk = [A.bf16(DC * 128), A.bf16(DC * 128)]
            uraw = A.f32(3 + L)
            uout = A.f32(L)
            dbuf = A.f32(L)
            dext = A.f32(SEXT)
            sext = A.f32(SEXT)
            dtmp = A.f32(TW)
            bcbuf = A.bf16(L)
            if on("vector"):
                E["vector"].memset(dext, 0.0)
                E["vector"].memset(uraw[:, 0:3], 0.0)
            dz_t = sig("vector", "aux")
            a2_b = C["aux2"]
            dext_b = C["aux"]
            ndelta_done = [0]

            for fi in range(17):
                is_u = fi < 8
                is_delta = 8 <= fi < 16
                dc_t = fi if is_u else fi - 8
                width = 64 if fi == 16 else 128
                fcol = fi * 128
                wb = wchk[fi % 2]
                wait("gpsimd", "aux2", a2_b + fi - 1)
                for k in range(DC):
                    dma("gpsimd", wb[:, k * 128:k * 128 + width],
                        t_winT[k * 128:(k + 1) * 128, fcol:fcol + width], "wld")
                wait("tensor", "wld", C["wld"])
                for tt in range(NT):
                    pt = mm_group(
                        lambda k, wb=wb, width=width:
                            wb[:, k * 128:k * 128 + width],
                        lambda k, tt=tt: xv[:, k, tt * TW:(tt + 1) * TW],
                        DC, width)
                    mtgt = C["smm"]
                    if is_u:
                        wait("vector", "smm", mtgt)
                        if on("vector"):
                            E["vector"].tensor_copy(
                                uraw[:, 3 + tt * TW:3 + (tt + 1) * TW],
                                pt[:, 0:TW])
                        pfree("vector")
                    elif is_delta:
                        wait("scalar", "smm", mtgt)
                        if tt == 0:
                            wait("scalar", "aux", dext_b + ndelta_done[0])
                        if on("scalar"):
                            s = E["scalar"]
                            s.activation(dtmp, pt[:, 0:TW], ACT.Exp)
                            s.activation(dbuf[:, tt * TW:(tt + 1) * TW], dtmp,
                                         ACT.Ln, bias=1.0)
                        pfree("scalar")
                    else:
                        wait("vector", "smm", mtgt)
                        if on("vector"):
                            E["vector"].tensor_copy(
                                bcbuf[0:64, tt * TW:(tt + 1) * TW],
                                pt[0:64, 0:TW])
                        pfree("vector")
                sig("tensor", "aux2")
                if is_u:
                    wait("vector", "ustore", C["ustore"])
                    if on("vector"):
                        v = E["vector"]
                        wj = convw_c[:, dc_t * 4:dc_t * 4 + 4]
                        v.tensor_scalar(uout, uraw[:, 0:L], wj[:, 0:1],
                                        convb_c[:, dc_t:dc_t + 1],
                                        ALU.mult, ALU.add)
                        for j in (1, 2, 3):
                            v.scalar_tensor_tensor(uout, uraw[:, j:j + L],
                                                   wj[:, j:j + 1], uout,
                                                   ALU.mult, ALU.add)
                    t = sig("vector", "evac")
                    wait("sync", "evac", t)
                    dma("sync", d_uscr[dc_t, :, 1:1 + L], uout, "ustore")
                elif is_delta:
                    wait("vector", "sfree", C["sfree"])
                    wait("vector", "aux", dz_t)
                    wait("vector", "ustore", C["ustore"])
                    de3 = dext.rearrange("p (c s) -> p c s", s=SEG)
                    if on("vector"):
                        v = E["vector"]
                        v.tensor_copy(
                            de3[:, 0:32, 1:1 + Q],
                            dbuf[:, 0:32 * Q].rearrange("p (c s) -> p c s", s=Q))
                        v.tensor_copy(dext[:, 32 * SEG + 1:32 * SEG + 1 + LASTW],
                                      dbuf[:, 32 * Q:L])
                    sig("vector", "aux")
                    ndelta_done[0] += 1
                    if on("vector"):
                        v = E["vector"]
                        v.tensor_tensor_scan(sext, resetf, dext, 0.0,
                                             ALU.mult, ALU.add)
                    t = sig("vector", "evac")
                    wait("sync", "evac", t)
                    dma("sync", d_sscr[dc_t], sext, "ustore")
                else:
                    t = C["sfree"]
                    wait("sync", "sfree", t)
                    dma("sync", d_bcscr[:, 0:L], bcbuf[0:64, :], "ustore")
            wait("sync", "ustore", C["ustore"])
            barrier()
            A.pop()
            A.pop()
            if MAX_PH <= 3:
                if CC_EARLY:
                    global_scalar_max(lnmax, s1col, is1col)
                    barrier()
                return C

            A.push()
            XM = [A.f32(CUBE), A.f32(CUBE)]
            Ep = [A.f32(CUBE), A.f32(CUBE)]
            Bu = A.f32(CUBE)
            Tb = A.f32(CUBE)
            Brep = [A.bf16(CUBE), A.bf16(CUBE)]
            Crep = [A.bf16(CUBE), A.bf16(CUBE)]
            sall = [A.f32(DC * SEG), A.f32(DC * SEG)]
            uall = [A.f32(DC * SEG), A.f32(DC * SEG)]
            ybuf = [A.f32(DC * Q), A.f32(DC * Q)]
            bflat = A.bf16(2 * CUBE)
            if on("vector"):
                E["vector"].memset(bflat, 0.0)
            bfz_t = sig("vector", "aux")
            wait("scalar", "aux", bfz_t)
            prev_bc_tgt = 0
            for c in range(NCH):
                pb = c % 2
                w = Q if c < 32 else LASTW
                wait("sync", "sdc", 8 * (c - 1))
                for dc in range(DC):
                    dma("sync", sall[pb][:, dc * SEG:(dc + 1) * SEG],
                        d_sscr[dc, :, c * SEG:(c + 1) * SEG], "cld")
                    dma("sync", uall[pb][:, dc * SEG:(dc + 1) * SEG],
                        d_uscr[dc, :, c * Q:c * Q + SEG], "cld")
                wait("scalar", "sdc", 8 * (c - 1))
                wait("scalar", "cld", prev_bc_tgt)
                b3 = bflat.rearrange("p (h n s) -> p h n s", h=2, n=NST)
                dma("scalar", b3[0:1, 0, :, 1:1 + Q],
                    d_bcscr[0:32, c * Q:c * Q + Q].unsqueeze(0), "cbf")
                dma("scalar", b3[0:1, 1, :, 1:1 + Q],
                    d_bcscr[32:64, c * Q:c * Q + Q].unsqueeze(0), "cbf")
                wait("scalar", "cbf", C["cbf"])
                dma("scalar", d_bcflat[0].unsqueeze(0), bflat[0:1, 0:CUBE], "cbf")
                dma("scalar", d_bcflat[1].unsqueeze(0),
                    bflat[0:1, CUBE:2 * CUBE], "cbf")
                wait("sync", "cbf", C["cbf"])
                dma("sync", Brep[pb],
                    d_bcflat[0].unsqueeze(0).broadcast_to([128, CUBE]), "cld")
                dma("sync", Crep[pb],
                    d_bcflat[1].unsqueeze(0).broadcast_to([128, CUBE]), "cld")
                prev_bc_tgt = C["cld"]
                cld_t = C["cld"]

                for dc in range(DC):
                    i = c * DC + dc
                    xb = XM[i % 2]
                    ep = Ep[i % 2]
                    xb3 = xb.rearrange("p (n s) -> p n s", n=NST)
                    wait("vector", "cld", cld_t)
                    wait("vector", "se", 2 * (i - 1))
                    if on("vector"):
                        E["vector"].tensor_tensor(
                            xb3,
                            sall[pb][:, dc * SEG:(dc + 1) * SEG]
                                .unsqueeze(1).broadcast_to([128, NST, SEG]),
                            acol[:, dc * NST:(dc + 1) * NST]
                                .unsqueeze(2).broadcast_to([128, NST, SEG]),
                            ALU.mult)
                    sig("vector", "sxm")
                    wait("scalar", "sxm", C["sxm"])
                    wait("scalar", "sdc", i - 1)
                    if on("scalar"):
                        s = E["scalar"]
                        s.activation(ep, xb, ACT.Exp)
                        s.activation(xb, xb, ACT.Exp, scale=-1.0)
                    sig("scalar", "se", 2)
                    bu3 = Bu.rearrange("p (n s) -> p n s", n=NST)
                    tb3 = Tb.rearrange("p (n s) -> p n s", n=NST)
                    if on("vector"):
                        v = E["vector"]
                        v.tensor_tensor(
                            bu3,
                            uall[pb][:, dc * SEG:(dc + 1) * SEG]
                                .unsqueeze(1).broadcast_to([128, NST, SEG]),
                            Brep[pb].rearrange("p (n s) -> p n s", n=NST),
                            ALU.mult)
                        v.tensor_copy(
                            bu3[:, :, 0:1],
                            hstate[:, dc * NST:(dc + 1) * NST].unsqueeze(2))
                    wait("vector", "se", 2 * (i + 1))
                    if on("vector"):
                        v = E["vector"]
                        v.tensor_tensor(Bu, Bu, xb, ALU.mult)
                        v.tensor_tensor_scan(Tb, resetf[:, 0:CUBE], Bu, 0.0,
                                             ALU.mult, ALU.add)
                        v.tensor_tensor(Tb, Tb, ep, ALU.mult)
                        if c < 32:
                            v.drain()
                            v.tensor_copy(
                                hstate[:, dc * NST:(dc + 1) * NST].unsqueeze(2),
                                tb3[:, :, Q:Q + 1])
                        v.tensor_tensor(Tb, Tb, Crep[pb], ALU.mult)
                        v.tensor_reduce(
                            ybuf[pb][:, dc * Q:(dc + 1) * Q],
                            tb3.transpose([0, 2, 1])[:, 1:SEG, :],
                            AX.X, ALU.add)
                    sig("vector", "sdc")
                wait("sync", "sdc", 8 * (c + 1))
                for dc in range(DC):
                    dma("sync", d_yscr[dc, :, c * Q:c * Q + w],
                        ybuf[pb][:, dc * Q:dc * Q + w], "ybst")
                wait("vector", "ybst", C["ybst"] - 8 * 16 if c >= 1 else 0)
            wait("sync", "ybst", C["ybst"])
            barrier()
            A.pop()
            if MAX_PH <= 4:
                if CC_EARLY:
                    global_scalar_max(lnmax, s1col, is1col)
                    barrier()
                return C

            A.push()
            ytile = A.f32(L)
            util = A.f32(L)
            xtil = A.f32(L)
            x2til = A.f32(L)
            vbf = A.bf16(DC * TW)
            sbufs5 = ([A.bf16(TW), A.bf16(TW)], [A.bf16(TW), A.bf16(TW)],
                      A.f32(TW), A.f32(TW), A.f32(TW))
            xlf_b = C["xlf"]
            for tt in range(NT):
                wait("sync", "xlf", xlf_b + tt)
                for dc in range(DC):
                    sp = slice(tt * TW, (tt + 1) * TW)
                    dma("sync", ytile[:, dc * TW:(dc + 1) * TW],
                        d_yscr[dc, :, sp], "ldx")
                    dma("sync", util[:, dc * TW:(dc + 1) * TW],
                        d_uscr[dc, :, 1 + tt * TW:1 + (tt + 1) * TW], "ldx")
                    dma("sync", xtil[:, dc * TW:(dc + 1) * TW],
                        t_x[dc * 128:(dc + 1) * 128, sp], "ldx")
                wait("vector", "ldx", C["ldx"])
                for dc in range(DC):
                    if on("vector"):
                        E["vector"].scalar_tensor_tensor(
                            vbf[:, dc * TW:(dc + 1) * TW],
                            util[:, dc * TW:(dc + 1) * TW],
                            dp_c[:, dc:dc + 1],
                            ytile[:, dc * TW:(dc + 1) * TW],
                            ALU.mult, ALU.add)
                t = sig("vector", "evac")
                wait("tensor", "evac", t)
                for do in range(DC):
                    pt = mm_group(
                        lambda k, do=do: woutTb[:, k * 1024 + do * 128:
                                                k * 1024 + (do + 1) * 128],
                        lambda k: vbf[:, k * TW:(k + 1) * TW], DC)
                    wait("vector", "smm", C["smm"])
                    if on("vector"):
                        E["vector"].tensor_add(
                            x2til[:, do * TW:(do + 1) * TW], pt[:, 0:TW],
                            xtil[:, do * TW:(do + 1) * TW])
                    pfree("vector")
                t = sig("vector", "evac")
                wait("sync", "evac", t)
                for dc in range(DC):
                    dma("sync", d_x2scr[dc, :, tt * TW:(tt + 1) * TW],
                        x2til[:, dc * TW:(dc + 1) * TW], "sstore")
                stats_tile(sbufs5,
                           lambda dc: x2til[:, dc * TW:(dc + 1) * TW], tt, 2)
                sig("vector", "xlf")
            wait("sync", "sstore", C["sstore"])
            barrier()
            A.pop()
            if MAX_PH <= 5:
                if CC_EARLY:
                    global_scalar_max(lnmax, s1col, is1col)
                    barrier()
                return C

            A6 = Carver(arena, core_end, ARENA)

            A6.push()
            m2rep = A6.f32(L)
            i2rep = A6.f32(L)
            x2b = [A6.f32(L), A6.f32(L)]
            lt0 = A6.f32(TW)
            lt1 = A6.f32(TW)
            colt = A6.f32(1)
            if KDBG:
                if on("vector"):
                    E["vector"].memset(m2rep, 7777.0)
                t = sig("vector", "aux")
                wait("sync", "aux", t)
                dma("sync", d_dbg[0, 0:16].unsqueeze(0), m2rep[0:1, 0:16], "sout")
            wait("sync", "rowdma", C["rowdma"])
            dma("sync", m2rep, d_rows[2, :].unsqueeze(0).broadcast_to([128, L]),
                "bcast")
            dma("sync", i2rep, d_rows[3, :].unsqueeze(0).broadcast_to([128, L]),
                "bcast")
            if KDBG:
                wait("sync", "bcast", C["bcast"])
                dma("sync", d_dbg[0, 16:32].unsqueeze(0), m2rep[0:1, 0:16], "sout")
                dma("sync", d_dbg[0, 32:48].unsqueeze(0), i2rep[0:1, 0:16], "sout")
            xlf_b = C["xlf"]
            if not SKIP_LNMAX:
                for tt in range(NT):
                    wait("sync", "xlf", xlf_b + tt - 1)
                    for dc in range(DC):
                        dma("sync", x2b[tt % 2][:, dc * TW:(dc + 1) * TW],
                            d_x2scr[dc, :, tt * TW:(tt + 1) * TW], "ldx")
                    wait("vector", "ldx", C["ldx"])
                    wait("vector", "bcast", C["bcast"])
                    for dc in range(DC):
                        if on("vector"):
                            v = E["vector"]
                            sp = slice(tt * TW, (tt + 1) * TW)
                            v.tensor_sub(lt0, x2b[tt % 2][:, dc * TW:(dc + 1) * TW],
                                         m2rep[:, sp])
                            v.tensor_tensor(lt1, lt0, i2rep[:, sp], ALU.mult)
                            v.reduce_max(colt, lt1, AX.X)
                            v.drain()
                            v.tensor_max(lnmax, lnmax, colt)
                    sig("vector", "xlf")
            if KDBG:
                t = sig("vector", "evac")
                wait("sync", "evac", t)
                dma("sync", d_dbg[5, 0:16].unsqueeze(0), tiny[0:1, :], "sout")
            s1_t = global_scalar_max(lnmax, s1col, is1col)
            if KDBG:
                wait("sync", "aux2", s1_t)
                dma("sync", d_dbg[5, 16:32].unsqueeze(0), tiny[0:1, :], "sout")
            barrier()
            A6.pop()
            if MAX_PH <= 6:
                return C

            A6.push()
            m2rep = A6.f32(L)
            i2rep = A6.f32(L)
            x2b = [A6.f32(L), A6.f32(L)]
            qt = A6.f32(TW)
            qx8 = [A6.u8(TW), A6.u8(TW)]
            wait("sync", "rowdma", C["rowdma"])
            dma("sync", m2rep, d_rows[2, :].unsqueeze(0).broadcast_to([128, L]),
                "bcast")
            dma("sync", i2rep, d_rows[3, :].unsqueeze(0).broadcast_to([128, L]),
                "bcast")
            xlf_b = C["xlf"]
            for tt in range(NT):
                wait("sync", "xlf", xlf_b + tt - 1)
                for dc in range(DC):
                    dma("sync", x2b[tt % 2][:, dc * TW:(dc + 1) * TW],
                        d_x2scr[dc, :, tt * TW:(tt + 1) * TW], "ldx")
                wait("vector", "ldx", C["ldx"])
                wait("vector", "bcast", C["bcast"])
                wait("vector", "aux2", s1_t)
                for dc in range(DC):
                    pb2 = dc % 2
                    wait("vector", "sstore", C["sstore"])
                    if on("vector"):
                        v = E["vector"]
                        sp = slice(tt * TW, (tt + 1) * TW)
                        v.tensor_sub(qt, x2b[tt % 2][:, dc * TW:(dc + 1) * TW],
                                     m2rep[:, sp])
                        v.tensor_tensor(qt, qt, i2rep[:, sp], ALU.mult)
                        v.tensor_scalar(qt, qt, is1col, MAGIC, ALU.mult, ALU.add)
                        v.tensor_scalar(qt, qt, MAGIC, 255.0, ALU.subtract,
                                        ALU.min)
                        v.tensor_scalar(qx8[pb2], qt, 0.0, None, ALU.max)
                    t = sig("vector", "evac")
                    wait("sync", "evac", t)
                    dma("sync", d_qxscr[dc, :, tt * TW:(tt + 1) * TW],
                        qx8[pb2], "sstore")
                sig("vector", "xlf")
            wait("sync", "sstore", C["sstore"])
            barrier()
            A6.pop()
            if MAX_PH <= 7:
                return C

            A6.push()
            qw1T = A6.bf16(DC * FF)
            qw2T = A6.bf16(NFT * 1024)
            wtile = [A6.f32(FF), A6.f32(FF)]
            invrep = A6.f32(FF)
            ev_b = C["evac"]
            for wt in range(NFT):
                pbw = wt % 2
                wait("sync", "evac", ev_b + wt - 1)
                dma("sync", wtile[pbw][:, 0:D],
                    t_w1[wt * 128:(wt + 1) * 128, :], "qld")
                wait("vector", "qld", C["qld"])
                if on("vector"):
                    E["vector"].reduce_max(sw1[:, wt:wt + 1], wtile[pbw][:, 0:D],
                                           AX.X, apply_absolute_value=True)
                sig("vector", "evac")
            ev_b = C["evac"]
            for wt in range(DC):
                pbw = wt % 2
                wait("sync", "evac", ev_b + wt - 1)
                dma("sync", wtile[pbw], t_w2[wt * 128:(wt + 1) * 128, :], "qld")
                wait("vector", "qld", C["qld"])
                if on("vector"):
                    E["vector"].reduce_max(sw2[:, wt:wt + 1], wtile[pbw],
                                           AX.X, apply_absolute_value=True)
                sig("vector", "evac")
            if on("vector"):
                v = E["vector"]
                v.drain()
                v.tensor_scalar(sw1, sw1, 1.0 / 127.0, None, ALU.mult)
                v.tensor_scalar(sw2, sw2, 1.0 / 127.0, None, ALU.mult)
                v.drain()
                v.drain()
                v.reciprocal(tmp32a, sw1)
                v.drain()
                v.reciprocal(tmp32b[:, 0:DC], sw2)
            t = sig("vector", "evac")
            wait("scalar", "evac", t)
            for wt in range(NFT):
                dma("scalar", d_small[0, wt * 128:(wt + 1) * 128].unsqueeze(1),
                    tmp32a[:, wt:wt + 1], "rowdma")
            for wt in range(DC):
                dma("scalar", d_small[1, wt * 128:(wt + 1) * 128].unsqueeze(1),
                    tmp32b[:, wt:wt + 1], "rowdma")
            wait("sync", "rowdma", C["rowdma"])
            dma("sync", invrep,
                d_small[0, :].unsqueeze(0).broadcast_to([128, FF]), "bcast")
            ev_b = C["evac"]
            for k in range(DC):
                pbw = k % 2
                wait("sync", "evac", ev_b + k - 1)
                dma("sync", wtile[pbw], t_w1T[k * 128:(k + 1) * 128, :], "qld")
                wait("vector", "qld", C["qld"])
                wait("vector", "bcast", C["bcast"])
                if on("vector"):
                    v = E["vector"]
                    wt_ = wtile[pbw]
                    v.tensor_tensor(wt_, wt_, invrep, ALU.mult)
                    v.tensor_scalar(wt_, wt_, MAGIC, MAGIC, ALU.add,
                                    ALU.subtract)
                    v.tensor_scalar(wt_, wt_, 127.0, -128.0, ALU.min, ALU.max)
                    v.tensor_copy(qw1T[:, k * FF:(k + 1) * FF], wt_)
                sig("vector", "evac")
            wait("sync", "evac", C["evac"])
            dma("sync", invrep[:, 0:D],
                d_small[1, 0:D].unsqueeze(0).broadcast_to([128, D]), "bcast")
            ev_b = C["evac"]
            for k2 in range(NFT):
                pbw = k2 % 2
                wait("sync", "evac", ev_b + k2 - 1)
                dma("sync", wtile[pbw][:, 0:D],
                    t_w2T[k2 * 128:(k2 + 1) * 128, :], "qld")
                wait("vector", "qld", C["qld"])
                wait("vector", "bcast", C["bcast"])
                if on("vector"):
                    v = E["vector"]
                    wt_ = wtile[pbw][:, 0:D]
                    v.tensor_tensor(wt_, wt_, invrep[:, 0:D], ALU.mult)
                    v.tensor_scalar(wt_, wt_, MAGIC, MAGIC, ALU.add,
                                    ALU.subtract)
                    v.tensor_scalar(wt_, wt_, 127.0, -128.0, ALU.min, ALU.max)
                    v.tensor_copy(qw2T[:, k2 * 1024:(k2 + 1) * 1024], wt_)
                sig("vector", "evac")
            if on("vector"):
                v = E["vector"]
                v.tensor_scalar(alpha1, sw1, s1col, None, ALU.mult)
                v.drain()
                v.reciprocal(tmp32a, alpha1)
                v.drain()
                v.tensor_tensor(tmp32b, b1c, tmp32a, ALU.mult)
                v.drain()
                v.tensor_scalar(tmp32b, tmp32b, MAGIC, MAGIC, ALU.add,
                                ALU.subtract)
                v.drain()
                v.tensor_tensor(bq1, tmp32b, alpha1, ALU.mult)
            t6c = sig("vector", "aux2")
            if KDBG:
                wait("sync", "aux2", t6c)
                dma("sync", d_dbg[5, 32:64].unsqueeze(0), sw1[0:1, :], "sout")
                dma("sync", d_dbg[5, 64:80].unsqueeze(0), tiny[0:1, :], "sout")
                dma("sync", d_dbg[5, 80:112].unsqueeze(0), alpha1[0:1, :], "sout")
                dma("gpsimd", d_dbg[5, 512:1024].unsqueeze(0),
                    qw1T[0:1, 0:512], "sout")
                dma("gpsimd", d_dbg[5, 1024:1536].unsqueeze(0),
                    qw2T[0:1, 0:512], "sout")
            A6.pop()
            A6.push()
            qw1T = A6.bf16(DC * FF)
            qw2T = A6.bf16(NFT * 1024)
            barrier()
            if MAX_PH <= 8:
                return C

            A6.push()
            qxu = A6.u8(DC * TW)
            qxb = A6.bf16(DC * TW)
            htile = [A6.f32(TW), A6.f32(TW)]
            hcol = A6.f32(1)
            aux_b = C["aux"]
            xlf_b = C["xlf"]
            for tt in range(NT):
                wait("sync", "xlf", xlf_b + tt)
                for dc in range(DC):
                    dma("sync", qxu[:, dc * TW:(dc + 1) * TW],
                        d_qxscr[dc, :, tt * TW:(tt + 1) * TW], "ldx")
                wait("vector", "ldx", C["ldx"])
                for dc in range(DC):
                    if on("vector"):
                        E["vector"].tensor_copy(qxb[:, dc * TW:(dc + 1) * TW],
                                                qxu[:, dc * TW:(dc + 1) * TW])
                t = sig("vector", "evac")
                wait("tensor", "evac", t)
                for ft in range(NFT):
                    j = tt * NFT + ft
                    pt = mm_group(
                        lambda k, ft=ft: qw1T[:, k * FF + ft * 128:
                                              k * FF + (ft + 1) * 128],
                        lambda k: qxb[:, k * TW:(k + 1) * TW], DC)
                    hb = j % 2
                    wait("scalar", "smm", C["smm"])
                    wait("scalar", "aux", aux_b + j - 1)
                    if on("scalar"):
                        E["scalar"].activation(htile[hb], pt[:, 0:TW], ACT.Relu,
                                               bias=bq1[:, ft:ft + 1],
                                               scale=alpha1[:, ft:ft + 1])
                    pfree("scalar")
                    t = sig("scalar", "aux2")
                    wait("vector", "aux2", t)
                    if on("vector"):
                        v = E["vector"]
                        v.reduce_max(hcol[:, 0:1], htile[hb], AX.X)
                        v.drain()
                        v.tensor_max(hmax, hmax, hcol[:, 0:1])
                    sig("vector", "aux")
                sig("vector", "xlf")
            s2_t = global_scalar_max(hmax, s2col, is2col)
            wait("vector", "aux2", s2_t)
            if on("vector"):
                v = E["vector"]
                v.tensor_scalar(scl2, alpha1, is2col, None, ALU.mult)
                v.tensor_scalar(bia2, bq1, is2col, MAGIC, ALU.mult, ALU.add)
                v.tensor_scalar(alpha2, sw2, s2col, None, ALU.mult)
                v.drain()
                v.reciprocal(tmp32b[:, 0:DC], alpha2)
                v.drain()
                v.tensor_tensor(tmp32a[:, 0:DC], b2c, tmp32b[:, 0:DC], ALU.mult)
                v.drain()
                v.tensor_scalar(tmp32a[:, 0:DC], tmp32a[:, 0:DC], MAGIC, MAGIC,
                                ALU.add, ALU.subtract)
                v.drain()
                v.tensor_tensor(bq2, tmp32a[:, 0:DC], alpha2, ALU.mult)
            t9 = sig("vector", "aux2")
            if KDBG:
                wait("sync", "aux2", t9)
                dma("sync", d_dbg[6, 200:216].unsqueeze(0), tiny[0:1, :], "sout")
                dma("sync", d_dbg[6, 216:224].unsqueeze(0), alpha2[0:1, :], "sout")
                dma("sync", d_dbg[6, 224:232].unsqueeze(0), bq2[0:1, :], "sout")
                dma("sync", d_dbg[6, 232:264].unsqueeze(0), scl2[0:1, :], "sout")
                dma("sync", d_dbg[6, 264:296].unsqueeze(0), bia2[0:1, :], "sout")
            barrier()
            A6.pop()
            if MAX_PH <= 9:
                return C

            A6.push()
            qxu = A6.u8(DC * TW)
            qxb = A6.bf16(DC * TW)
            qh = A6.bf16(NFT * TW)
            x2til = A6.f32(L)
            t1 = [A6.f32(TW), A6.f32(TW)]
            t2 = [A6.f32(TW), A6.f32(TW)]
            ot = [A6.f32(TW), A6.f32(TW)]
            xlf_b = C["xlf"]
            for tt in range(NT):
                wait("sync", "xlf", xlf_b + tt)
                for dc in range(DC):
                    dma("sync", qxu[:, dc * TW:(dc + 1) * TW],
                        d_qxscr[dc, :, tt * TW:(tt + 1) * TW], "ldx")
                    dma("sync", x2til[:, dc * TW:(dc + 1) * TW],
                        d_x2scr[dc, :, tt * TW:(tt + 1) * TW], "ldx")
                wait("vector", "ldx", C["ldx"])
                for dc in range(DC):
                    if on("vector"):
                        E["vector"].tensor_copy(qxb[:, dc * TW:(dc + 1) * TW],
                                                qxu[:, dc * TW:(dc + 1) * TW])
                t = sig("vector", "evac")
                wait("tensor", "evac", t)
                aux_b = C["aux"]
                for ft in range(NFT):
                    pt = mm_group(
                        lambda k, ft=ft: qw1T[:, k * FF + ft * 128:
                                              k * FF + (ft + 1) * 128],
                        lambda k: qxb[:, k * TW:(k + 1) * TW], DC)
                    hb = ft % 2
                    wait("scalar", "smm", C["smm"])
                    wait("scalar", "aux", aux_b + ft - 1)
                    if on("scalar"):
                        E["scalar"].activation(t1[hb], pt[:, 0:TW], ACT.Identity,
                                               bias=bia2[:, ft:ft + 1],
                                               scale=scl2[:, ft:ft + 1])
                    pfree("scalar")
                    t = sig("scalar", "aux2")
                    wait("vector", "aux2", t)
                    if on("vector"):
                        v = E["vector"]
                        v.tensor_scalar(t1[hb], t1[hb], MAGIC, 255.0,
                                        ALU.subtract, ALU.min)
                        v.tensor_scalar(qh[:, ft * TW:(ft + 1) * TW], t1[hb],
                                        0.0, None, ALU.max)
                    sig("vector", "aux")
                    sig("vector", "qhs")
                wait("tensor", "qhs", C["qhs"])
                aux_b = C["aux"]
                for do in range(DC):
                    pt = mm_group(
                        lambda k2, do=do: qw2T[:, k2 * 1024 + do * 128:
                                               k2 * 1024 + (do + 1) * 128],
                        lambda k2: qh[:, k2 * TW:(k2 + 1) * TW], NFT)
                    ob = do % 2
                    wait("scalar", "smm", C["smm"])
                    wait("scalar", "aux", aux_b + do - 1)
                    if on("scalar"):
                        E["scalar"].activation(t2[ob], pt[:, 0:TW], ACT.Identity,
                                               bias=bq2[:, do:do + 1],
                                               scale=alpha2[:, do:do + 1])
                    pfree("scalar")
                    t = sig("scalar", "aux2")
                    wait("vector", "aux2", t)
                    wait("vector", "sout", C["sout"])
                    if on("vector"):
                        E["vector"].tensor_add(ot[ob], t2[ob],
                                               x2til[:, do * TW:(do + 1) * TW])
                    sig("vector", "aux")
                    t = sig("vector", "evac")
                    wait("sync", "evac", t)
                    dma("sync", t_out[do * 128:(do + 1) * 128,
                                      tt * TW:(tt + 1) * TW], ot[ob], "sout")
                sig("vector", "xlf")
            wait("sync", "sout", C["sout"])
            barrier()
            A6.pop()
            return C

        def handles(**kw):
            d = {k: None for k in ENGS}
            d.update(kw)
            return d

        @block.sync
        def _(h):
            make_plan("sync", handles(sync=h))

        @block.vector
        def _(h):
            make_plan("vector", handles(vector=h))

        @block.scalar
        def _(h):
            make_plan("scalar", handles(scalar=h))

        @block.tensor
        def _(h):
            make_plan("tensor", handles(tensor=h))

        @block.gpsimd
        def _(h):
            make_plan("gpsimd", handles(gpsimd=h))

    return nc


_NC_CACHE = []


def kernel(L0_attn_x, L0_attn_W_in, L0_attn_W_out, L0_attn_conv_w,
           L0_attn_conv_b, L0_attn_A_log, L0_attn_D_param,
           L0_ffn_w1, L0_ffn_b1, L0_ffn_w2, L0_ffn_b2):
    x = np.asarray(L0_attn_x, dtype=np.float32)
    W_in = np.asarray(L0_attn_W_in, dtype=np.float32)
    W_out = np.asarray(L0_attn_W_out, dtype=np.float32)
    conv_w = np.asarray(L0_attn_conv_w, dtype=np.float32)
    conv_b = np.asarray(L0_attn_conv_b, dtype=np.float32)
    A_log = np.asarray(L0_attn_A_log, dtype=np.float32)
    D_param = np.asarray(L0_attn_D_param, dtype=np.float32)
    w1 = np.asarray(L0_ffn_w1, dtype=np.float32)
    b1 = np.asarray(L0_ffn_b1, dtype=np.float32)
    w2 = np.asarray(L0_ffn_w2, dtype=np.float32)
    b2 = np.asarray(L0_ffn_b2, dtype=np.float32)

    common = {
        "winT": np.ascontiguousarray(W_in[:2112].T),
        "woutT": np.ascontiguousarray(W_out.T),
        "w1": np.ascontiguousarray(w1),
        "w1T": np.ascontiguousarray(w1.T),
        "w2": np.ascontiguousarray(w2),
        "w2T": np.ascontiguousarray(w2.T),
        "alogT": np.ascontiguousarray(A_log.T),
        "convw": np.ascontiguousarray(conv_w.reshape(D, 4)),
        "convb": np.ascontiguousarray(conv_b),
        "dp": np.ascontiguousarray(D_param),
        "b1": np.ascontiguousarray(b1),
        "b2": np.ascontiguousarray(b2),
        "onesbf": np.ones((128, 1), dtype=ml_dtypes.bfloat16),
        "idf": np.eye(128, dtype=np.float32),
    }
    in_maps = []
    for b in range(8):
        m = dict(common)
        m["x"] = np.ascontiguousarray(x[b].T)
        in_maps.append(m)

    if not _NC_CACHE:
        _NC_CACHE.append(build_program())
    nc = _NC_CACHE[0]
    res = run_bass_kernel_spmd(nc, in_maps, core_ids=list(range(8)))
    if KDBG:
        kernel._dbg = res.results
    out = np.stack([np.ascontiguousarray(np.asarray(r["out"]).T)
                    for r in res.results])
    return out
